# version 01 (final)
# baseline (speedup 1.0000x reference)
import numpy as np
import ml_dtypes
import concourse.bass as bass
import concourse.mybir as mybir
from concourse.bass_utils import run_bass_kernel_spmd

F32 = mybir.dt.float32
BF16 = mybir.dt.bfloat16
U8 = mybir.dt.uint8
I32 = mybir.dt.int32
AF = mybir.ActivationFunctionType
ALU = mybir.AluOpType
AX = mybir.AxisListType
BF = ml_dtypes.bfloat16

NCORES = 8
T = 2048
D = 1024
NCH = T // 128


def _view(ap, off, dims):
    return bass.AP(ap.tensor, ap.offset + off, [ap.ap[0]] + dims)


class Prog:
    def __init__(self):
        self.nc = bass.Bass("TRN2", target_bir_lowering=False)
        nc = self.nc
        self.sem = nc.semaphore().__enter__()
        self.val = 0
        self.ops = []
        self.handles = []
        self.in_arrs = {}
        self.out_names = []

    def _do(self, ename, fn, dma, a, k):
        wait_val = self.val
        inc = 16 if dma else 1

        def emit(eng, fn=fn, a=a, k=k, wait_val=wait_val, inc=inc):
            eng.wait_ge(self.sem, wait_val)
            fn(*a, **k).then_inc(self.sem, inc)

        self.ops.append((ename, emit))
        self.val += inc

    def dma(self, out, in_):
        self._do('sync', self.nc.sync.dma_start, True, (), dict(out=out, in_=in_))

    def mm(self, out, lhsT, rhs, start=True, stop=True):
        self._do('tensor', self.nc.tensor.matmul, False, (out, lhsT, rhs),
                 dict(start=start, stop=stop, skip_group_check=True))

    def tr(self, out, in_, ident):
        self._do('tensor', self.nc.tensor.transpose, False, (out, in_, ident), {})

    def ccol(self, v, r0=0, r1=128):
        if not hasattr(self, '_ccols'):
            self._ccols = {}
        if v not in self._ccols:
            t = self.sb("ccol_%d" % len(self._ccols), [128, 1], F32)
            self.memset(t[:], float(v))
            self._ccols[v] = t
        return self._ccols[v][r0:r1, 0:1]

    def act(self, out, in_, func, **k):
        if 'bias' in k and isinstance(k['bias'], float):
            k['bias'] = self.ccol(k['bias'])
        self._do('scalar', self.nc.scalar.activation, False, (),
                 dict(out=out, in_=in_, func=func, **k))

    def scopy(self, out, in_):
        self._do('scalar', self.nc.scalar.copy, False, (), dict(out=out, in_=in_))

    def vcopy(self, out, in_):
        self._do('vector', self.nc.vector.tensor_copy, False, (), dict(out=out, in_=in_))

    def tt(self, out, in0, in1, op):
        self._do('vector', self.nc.vector.tensor_tensor, False, (),
                 dict(out=out, in0=in0, in1=in1, op=op))

    def ts(self, out, in0, s1, op0, s2=None, op1=ALU.bypass, accum_out=None):
        self._do('vector', self.nc.vector.tensor_scalar, False, (),
                 dict(out=out, in0=in0, scalar1=s1, scalar2=s2, op0=op0, op1=op1,
                      accum_out=accum_out))

    def stt(self, out, in0, s, in1, op0, op1):
        self._do('vector', self.nc.vector.scalar_tensor_tensor, False, (),
                 dict(out=out, in0=in0, scalar=s, in1=in1, op0=op0, op1=op1))

    def rsum(self, out, in_):
        self._do('vector', self.nc.vector.reduce_sum, False, (),
                 dict(out=out, in_=in_, axis=AX.X))

    def rmax(self, out, in_):
        self._do('vector', self.nc.vector.reduce_max, False, (),
                 dict(out=out, in_=in_, axis=AX.X))

    def recip(self, out, in_):
        self._do('vector', self.nc.vector.reciprocal, False, (), dict(out=out, in_=in_))

    def sel(self, out, mask, t, f):
        self._do('vector', self.nc.vector.select, False, (),
                 dict(out=out, mask=mask, on_true=t, on_false=f))

    def memset(self, ap, v):
        self._do('vector', self.nc.vector.memset, False, (ap, v), {})

    def fin(self):
        v = self.val
        self.ops.append(('sync', lambda eng, v=v: eng.wait_ge(self.sem, v)))

    def inp(self, name, arr, dtype=None):
        a0 = arr[0] if isinstance(arr, list) else arr
        dt = dtype or {np.dtype(np.float32): F32, np.dtype(np.int32): I32,
                       np.dtype(BF): BF16}[a0.dtype]
        h = self.nc.dram_tensor(name, list(a0.shape), dt, kind="ExternalInput").ap()
        self.in_arrs[name] = arr
        return h

    def outp(self, name, shape, dt=F32):
        h = self.nc.dram_tensor(name, list(shape), dt, kind="ExternalOutput").ap()
        self.out_names.append(name)
        return h

    def scratch(self, name, shape, dt=F32):
        return self.nc.dram_tensor(name, list(shape), dt, kind="Internal").ap()

    def sb(self, name, shape, dt=F32):
        h = self.nc.sbuf_tensor(name, list(shape), dt)
        t = h.__enter__()
        self.handles.append(h)
        return t

    def ps(self, name, shape, dt=F32):
        h = self.nc.psum_tensor(name, list(shape), dt)
        t = h.__enter__()
        self.handles.append(h)
        return t

    def consts(self):
        eye = np.eye(128, dtype=np.float32)
        triu = np.triu(np.ones((128, 128), np.float32))
        self.ident_bf = self.sb("ident_bf", [128, 128], BF16)
        self.ident_f = self.sb("ident_f", [128, 128], F32)
        self.triu_bf = self.sb("triu_bf", [128, 128], BF16)
        self.ones_c_bf = self.sb("ones_c_bf", [128, 1], BF16)
        self.ones_r_bf = self.sb("ones_r_bf", [1, 128], BF16)
        self.ones_r_f = self.sb("ones_r_f", [1, 128], F32)
        i1 = self.inp("c_eye", eye)
        i2 = self.inp("c_eyebf", eye.astype(BF))
        i3 = self.inp("c_triubf", triu.astype(BF))
        self.dma(self.ident_f[:], i1)
        self.dma(self.ident_bf[:], i2)
        self.dma(self.triu_bf[:], i3)
        oc = self.inp("c_onescol", np.ones((128, 1), BF))
        orw = self.inp("c_onesrow", np.ones((1, 128), BF))
        orf = self.inp("c_onesrowf", np.ones((1, 128), np.float32))
        self.dma(self.ones_c_bf[:], oc)
        self.dma(self.ones_r_bf[:], orw)
        self.dma(self.ones_r_f[:], orf)

    def load_bf(self, name, hbm, R, C):
        nr = R // 128
        stg = self.sb(name + "_stg", [128, C], F32)
        dst = self.sb(name, [128, nr * C], BF16)
        for j in range(nr):
            self.dma(stg[:], hbm[j * 128:(j + 1) * 128, :])
            self.scopy(dst[:, j * C:(j + 1) * C], stg[:])
        return dst

    def transpose_big(self, name, src, R, C, psum_t):
        nr, ncl = R // 128, C // 128
        dst = self.sb(name, [128, ncl * R], BF16)
        for j in range(nr):
            for k in range(ncl):
                self.tr(psum_t[:], src[:, j * C + k * 128: j * C + (k + 1) * 128],
                        self.ident_bf[:])
                self.vcopy(dst[:, k * R + j * 128: k * R + (j + 1) * 128], psum_t[:])
        return dst

    def ln(self, x_sb, xn_sb, xn_bf, scr, col, cols=D):
        self.rsum(col[:, 0:1], x_sb[:, 0:cols])
        self.ts(col[:, 1:2], col[:, 0:1], -1.0 / cols, ALU.mult)
        self.ts(xn_sb[:, 0:cols], x_sb[:, 0:cols], col[:, 1:2], ALU.add)
        self.act(scr[:, 0:cols], xn_sb[:, 0:cols], AF.Square, accum_out=col[:, 2:3])
        self.act(col[:, 3:4], col[:, 2:3], AF.Ln, scale=1.0 / cols, bias=1e-5)
        self.act(col[:, 2:3], col[:, 3:4], AF.Exp, scale=-0.5)
        self.ts(xn_sb[:, 0:cols], xn_sb[:, 0:cols], col[:, 2:3], ALU.mult)
        if xn_bf is not None:
            self.scopy(xn_bf[:, 0:cols], xn_sb[:, 0:cols])

    def tr_chunk(self, xnT, xn_bf, psum_t, n=8, dst_off=0):
        for k in range(n):
            self.tr(psum_t[:], xn_bf[:, k * 128:(k + 1) * 128], self.ident_bf[:])
            self.vcopy(xnT[:, dst_off + k * 128:dst_off + (k + 1) * 128], psum_t[:])

    def finish(self):
        nc = self.nc
        self.fin()
        with nc.Block() as block:
            @block.sync
            def _(eng):
                for n, e in self.ops:
                    if n == 'sync':
                        e(eng)

            @block.vector
            def _(eng):
                for n, e in self.ops:
                    if n == 'vector':
                        e(eng)

            @block.scalar
            def _(eng):
                for n, e in self.ops:
                    if n == 'scalar':
                        e(eng)

            @block.tensor
            def _(eng):
                for n, e in self.ops:
                    if n == 'tensor':
                        e(eng)
        for h in reversed(self.handles):
            h.__exit__(None, None, None)

    def run(self):
        self.finish()
        in_maps = []
        for c in range(NCORES):
            m = {}
            for k, v in self.in_arrs.items():
                m[k] = v[c] if isinstance(v, list) else v
            in_maps.append(m)
        res = run_bass_kernel_spmd(self.nc, in_maps, core_ids=list(range(NCORES)))
        return res.results


def shard(x):
    return [np.ascontiguousarray(x[i]) for i in range(NCORES)]


def build_L0a(xs, W0T_h, wout_h):
    p = Prog()
    x_h = p.inp("x", xs)
    w0_h = p.inp("w0t", W0T_h)
    wo_h = p.inp("wout", wout_h)
    y_h = p.outp("y", [T, D])
    p.consts()
    W0 = p.load_bf("W0", w0_h, 1024, 3072)
    WO = p.load_bf("WO", wo_h, 1024, 1024)

    x_sb = p.sb("x_sb", [128, D])
    xn = p.sb("xn", [128, D])
    xn_bf = p.sb("xn_bf", [128, D], BF16)
    scr = p.sb("scr", [128, D])
    col = p.sb("col", [128, 4])
    xnT = p.sb("xnT", [128, D], BF16)
    q_sb = p.sb("q_sb", [128, 1024])
    kexp = p.sb("kexp", [128, 1024], BF16)
    v_bf = p.sb("v_bf", [128, 1024], BF16)
    carry = p.sb("carry", [1, 1024], BF16)
    qsT = p.sb("qsT", [64, 128], BF16)
    kT = p.sb("kT", [64, 128], BF16)
    scm = p.sb("scm", [128, 128], BF16)
    S_bf = p.sb("S_bf", [64, 1024], BF16)
    y_bf = p.sb("y_bf", [128, 1024], BF16)
    yT = p.sb("yT", [128, 1024], BF16)
    e_scr = p.sb("e_scr", [128, 64])
    den = p.sb("den", [128, 64])
    out_sb = p.sb("out_sb", [128, D])

    ps_q = p.ps("ps_q", [128, 512])
    ps_car = p.ps("ps_car", [1, 1024])
    ps_t = p.ps("ps_t", [128, 128], BF16)
    ps_small = p.ps("ps_small", [128, 128])
    ps_sc = p.ps("ps_sc", [128, 128])
    ps_SA = p.ps("ps_SA", [64, 1024])
    ps_kn = ps_small[:, 0:64]
    ps_yh = ps_small[:, 64:128]

    for c in range(NCH):
        r0 = c * 128
        p.dma(x_sb[:], x_h[r0:r0 + 128, :])
        p.ln(x_sb, xn, xn_bf, scr, col)
        p.tr_chunk(xnT, xn_bf, ps_t)
        for g in range(6):
            for k in range(8):
                p.mm(ps_q[:], xnT[:, k * 128:(k + 1) * 128],
                     W0[:, k * 3072 + g * 512: k * 3072 + (g + 1) * 512],
                     start=(k == 0), stop=(k == 7))
            if g < 2:
                p.vcopy(q_sb[:, g * 512:(g + 1) * 512], ps_q[:])
            elif g < 4:
                p.act(scr[:, 0:512], ps_q[:], AF.Exp, scale=0.125)
                p.ts(kexp[:, (g - 2) * 512:(g - 1) * 512], scr[:, 0:512], 1e-6, ALU.add)
            else:
                p.vcopy(v_bf[:, (g - 4) * 512:(g - 3) * 512], ps_q[:])
        for h in range(16):
            sl = slice(64 * h, 64 * h + 64)
            if c == 0:
                p.memset(carry[0:1, sl], 0.0)
            else:
                p.vcopy(carry[0:1, sl], ps_car[0:1, sl])
            p.mm(ps_kn, p.triu_bf[:], kexp[:, sl], start=True, stop=False)
            p.mm(ps_kn, p.ones_r_bf[:], carry[0:1, sl], start=False, stop=True)
            p.mm(ps_car[0:1, sl], p.ones_c_bf[:], kexp[:, sl],
                 start=(c == 0), stop=(c == NCH - 1))
            p.rmax(col[:, 0:1], q_sb[:, sl])
            p.ts(col[:, 1:2], col[:, 0:1], -0.125, ALU.mult)
            p.act(e_scr[:], q_sb[:, sl], AF.Exp, scale=0.125, bias=col[:, 1:2])
            p.rsum(col[:, 2:3], e_scr[:])
            p.ts(den[:], ps_kn, col[:, 2:3], ALU.mult)
            p.recip(den[:], den[:])
            p.tt(scr[:, 0:64], e_scr[:], den[:], ALU.mult)
            p.scopy(y_bf[:, sl], scr[:, 0:64])
            p.tr(ps_t[0:64, 0:128], y_bf[:, sl], p.ident_bf[:])
            p.vcopy(qsT[:], ps_t[0:64, 0:128])
            p.tr(ps_t[0:64, 0:128], kexp[:, sl], p.ident_bf[:])
            p.vcopy(kT[:], ps_t[0:64, 0:128])
            p.mm(ps_sc[:], kT[:], qsT[:], start=True, stop=True)
            p.tt(scm[:], ps_sc[:], p.triu_bf[:], ALU.mult)
            p.mm(ps_yh, scm[:], v_bf[:, sl], start=True, stop=(c == 0))
            if c > 0:
                p.mm(ps_yh, qsT[:], S_bf[0:64, sl], start=False, stop=True)
            p.mm(ps_SA[0:64, sl], kexp[:, sl], v_bf[:, sl],
                 start=(c == 0), stop=(c == NCH - 1))
            p.vcopy(S_bf[0:64, sl], ps_SA[0:64, sl])
            p.vcopy(y_bf[:, sl], ps_yh)
        p.tr_chunk(yT, y_bf, ps_t)
        for j2 in range(2):
            for k in range(8):
                p.mm(ps_q[:], yT[:, k * 128:(k + 1) * 128],
                     WO[:, k * 1024 + j2 * 512: k * 1024 + (j2 + 1) * 512],
                     start=(k == 0), stop=(k == 7))
            p.tt(out_sb[:, j2 * 512:(j2 + 1) * 512], ps_q[:],
                 x_sb[:, j2 * 512:(j2 + 1) * 512], ALU.add)
        p.dma(y_h[r0:r0 + 128, :], out_sb[:])
    return p


def fp4_quant(p, src_f32, dst_bf, scr, bmax, F):
    nb = F // 16
    p.act(scr[:, 0:F], src_f32, AF.Abs)
    p.tt(bmax[:, 0:nb], scr[:, 0:F:16], scr[:, 1:F:16], ALU.max)
    for j in range(2, 16):
        p.tt(bmax[:, 0:nb], bmax[:, 0:nb], scr[:, j:F:16], ALU.max)
    p.ts(bmax[:, nb:2 * nb], bmax[:, 0:nb], 1.0 / 6.0, ALU.mult, 1e-12, ALU.max)
    p.recip(bmax[:, 2 * nb:3 * nb], bmax[:, nb:2 * nb])
    p.ts(bmax[:, 0:nb], bmax[:, nb:2 * nb], 0.5, ALU.mult)
    inv_v = _view(bmax[:], 2 * nb, [[1, nb], [0, 16]])
    sc2_v = _view(bmax[:], 0, [[1, nb], [0, 16]])
    p.tt(scr[:, 0:F], scr[:, 0:F], inv_v, ALU.mult)
    thrs = [0.25, 0.75, 1.25, 1.75, 2.5, 2.5, 3.5, 3.5, 5.0, 5.0, 5.0, 5.0]
    p.ts(dst_bf, scr[:, 0:F], thrs[0], ALU.is_gt)
    for thr in thrs[1:]:
        p.stt(dst_bf, scr[:, 0:F], thr, dst_bf, ALU.is_gt, ALU.add)
    p.act(scr[:, 0:F], src_f32, AF.Sign)
    p.tt(dst_bf, dst_bf, scr[:, 0:F], ALU.mult)
    p.tt(dst_bf, dst_bf, sc2_v, ALU.mult)


def build_fp4_ffn(xs, ys, wg_h_a, wu_h_a, wd_h_a, tag):
    p = Prog()
    x_h = p.inp("x", xs)
    y_h = p.inp("yin", ys)
    wg_h = p.inp("wg", wg_h_a)
    wu_h = p.inp("wu", wu_h_a)
    wd_h = p.inp("wd", wd_h_a)
    o_h = p.outp("y", [T, D])
    p.consts()
    I = 2816
    scrw = p.sb("scrw", [128, I])
    bmax = p.sb("bmax", [128, 3 * (I // 16)])
    ps_t = p.ps("ps_t", [128, 128], BF16)
    ps_g = p.ps("ps_g", [128, 512])
    ps_u = p.ps("ps_u", [128, 512])

    stg = p.sb("stg", [128, I], F32)
    hq_bf = p.sb("hq_bf", [128, I], BF16)
    hqT = p.sb("hqT", [128, I], BF16)
    wgT = p.sb("wgT", [128, 8 * I], BF16)
    wuT = p.sb("wuT", [128, 8 * I], BF16)
    wdT = p.sb("wdT", [128, 22 * 1024], BF16)

    def quant_weight_T(hbm, R, C, dstT):
        nr, ncl = R // 128, C // 128
        for j in range(nr):
            p.dma(stg[:, 0:C], hbm[j * 128:(j + 1) * 128, :])
            fp4_quant(p, stg[:, 0:C], hq_bf[:, 0:C], scrw, bmax, C)
            for k in range(ncl):
                p.tr(ps_t[:], hq_bf[:, k * 128:(k + 1) * 128], p.ident_bf[:])
                p.vcopy(dstT[:, k * R + j * 128: k * R + (j + 1) * 128], ps_t[:])

    quant_weight_T(wg_h, I, 1024, wgT)
    quant_weight_T(wu_h, I, 1024, wuT)
    quant_weight_T(wd_h, 1024, I, wdT)

    x_sb = p.sb("x_sb", [128, D])
    y_sb = p.sb("y_sb", [128, D])
    xn = p.sb("xn", [128, D])
    col = p.sb("col", [128, 4])
    out_sb = p.sb("out_sb", [128, D])
    xqT = hqT[:, 0:1024]
    xq_bf = hqT[:, 1024:2048]

    GRP = [(0, 512), (512, 1024), (1024, 1536), (1536, 2048), (2048, 2560), (2560, 2816)]
    for c in range(NCH):
        r0 = c * 128
        p.dma(x_sb[:], x_h[r0:r0 + 128, :])
        p.dma(y_sb[:], y_h[r0:r0 + 128, :])
        p.ln(x_sb, xn, None, scrw, col)
        fp4_quant(p, xn[:], xq_bf, scrw, bmax, 1024)
        for k in range(8):
            p.tr(ps_t[:], xq_bf[:, k * 128:(k + 1) * 128], p.ident_bf[:])
            p.vcopy(xqT[:, k * 128:(k + 1) * 128], ps_t[:])
        for (a, b) in GRP:
            for k in range(8):
                p.mm(ps_g[:, 0:b - a], xqT[:, k * 128:(k + 1) * 128],
                     wgT[:, k * I + a: k * I + b], start=(k == 0), stop=(k == 7))
            for k in range(8):
                p.mm(ps_u[:, 0:b - a], xqT[:, k * 128:(k + 1) * 128],
                     wuT[:, k * I + a: k * I + b], start=(k == 0), stop=(k == 7))
            p.act(stg[:, a:b], ps_g[:, 0:b - a], AF.Silu)
            p.tt(stg[:, a:b], stg[:, a:b], ps_u[:, 0:b - a], ALU.mult)
        fp4_quant(p, stg[:], hq_bf[:], scrw, bmax, I)
        for k in range(22):
            p.tr(ps_t[:], hq_bf[:, k * 128:(k + 1) * 128], p.ident_bf[:])
            p.vcopy(hqT[:, k * 128:(k + 1) * 128], ps_t[:])
        for j2 in range(2):
            for k in range(22):
                p.mm(ps_g[:], hqT[:, k * 128:(k + 1) * 128],
                     wdT[:, k * 1024 + j2 * 512: k * 1024 + (j2 + 1) * 512],
                     start=(k == 0), stop=(k == 21))
            p.tt(out_sb[:, j2 * 512:(j2 + 1) * 512], ps_g[:],
                 y_sb[:, j2 * 512:(j2 + 1) * 512], ALU.add)
        p.dma(o_h[r0:r0 + 128, :], out_sb[:])
    return p


def build_L1a(xs, WzT_a, WxmT_a, WBT_a, WCT_a, wdtT_a, convw_a, Alog_a, Dp_a,
              normw_a, OPT_a, maskneg_a, blk64_a):
    p = Prog()
    x_h = p.inp("x", xs)
    wz_h = p.inp("wzt", WzT_a)
    wxm_h = p.inp("wxmt", WxmT_a)
    wb_h = p.inp("wbt", WBT_a)
    wc_h = p.inp("wct", WCT_a)
    wdt_h = p.inp("wdtt", wdtT_a)
    cw_h = p.inp("convw", convw_a)
    al_h = p.inp("alog", Alog_a)
    dp_h = p.inp("dp", Dp_a)
    nw_h = p.inp("normw", normw_a)
    op_h = p.inp("opt", OPT_a)
    mn_h = p.inp("maskneg", maskneg_a)
    bk_h = p.inp("blk64", blk64_a)
    y_h = p.outp("y", [T, D])
    p.consts()
    DI = 2048

    ps_t = p.ps("ps_t", [128, 128], BF16)
    ps_f = p.ps("ps_f", [128, 128])
    ps_q = p.ps("ps_q", [128, 512])
    ps_big = p.ps("ps_big", [128, 2048])

    WB = p.load_bf("WB", wb_h, 1024, 128)
    WC = p.load_bf("WC", wc_h, 1024, 128)
    Wdt = p.load_bf("Wdt", wdt_h, 1024, 1)
    wstg = p.sb("wstg", [128, 2048], F32)
    wbuf = p.sb("wbuf", [128, 2048], BF16)
    maskneg = p.sb("maskneg_sb", [128, 4096], BF16)
    p.dma(maskneg[:], mn_h)
    blk64 = p.sb("blk64_sb", [128, 128])
    p.dma(blk64[:], bk_h)
    convw = p.sb("convw_sb", [128, 64])
    for j in range(16):
        p.dma(convw[:, j * 4:(j + 1) * 4], cw_h[j * 128:(j + 1) * 128, :])
    A_col = p.sb("A_col", [128, 2])
    p.dma(A_col[:, 0:1], al_h.rearrange("(a b) -> a b", b=1))
    p.act(A_col[:, 1:2], A_col[:, 0:1], AF.Exp)
    p.ts(A_col[:, 0:1], A_col[:, 1:2], -1.0, ALU.mult)
    dp_row = p.sb("dp_row", [1, 2048])
    p.dma(dp_row[:], dp_h.rearrange("(a b) -> a b", a=1))
    nw_row = p.sb("nw_row", [1, 2048])
    p.dma(nw_row[:], nw_h.rearrange("(a b) -> a b", a=1))
    Dp_bc = p.sb("Dp_bc", [128, 2048], BF16)
    Nw_bc = p.sb("Nw_bc", [128, 2048], BF16)
    for g in range(4):
        p.mm(ps_q[:], p.ones_r_f[:], dp_row[0:1, g * 512:(g + 1) * 512])
        p.vcopy(Dp_bc[:, g * 512:(g + 1) * 512], ps_q[:])
        p.mm(ps_q[:], p.ones_r_f[:], nw_row[0:1, g * 512:(g + 1) * 512])
        p.vcopy(Nw_bc[:, g * 512:(g + 1) * 512], ps_q[:])

    x_sb = p.sb("x_sb", [128, D])
    xn = p.sb("xn", [128, D])
    xn_bf = p.sb("xn_bf", [128, D], BF16)
    scr = p.sb("scr", [128, 2048])
    col = p.sb("col", [128, 8])
    xnT = p.sb("xnT", [128, D], BF16)
    z_bf = p.sb("z_bf", [128, DI], BF16)
    xm_bf = p.sb("xm_bf", [128, DI], BF16)
    convbuf = p.sb("convbuf", [128, 16 * 132], BF16)
    xmc_bf = p.sb("xmc_bf", [128, DI], BF16)
    xmcT_bf = p.sb("xmcT_bf", [128, 128], BF16)
    bT_bf = p.sb("bT_bf", [128, 128], BF16)
    cT_bf = p.sb("cT_bf", [128, 128], BF16)
    dts = p.sb("dts", [128, 4])
    U_sb = p.sb("U_sb", [128, 128])
    Tb = p.sb("Tb", [1, 128])
    dts_bc = p.sb("dts_bc", [128, 128])
    Ec = p.sb("Ec", [128, 128])
    Edec = p.sb("Edec", [128, 128])
    CtT = p.sb("CtT", [128, 128], BF16)
    BtT = p.sb("BtT", [128, 128], BF16)
    BtTt = p.sb("BtTt", [128, 128], BF16)
    e_end = p.sb("e_end", [128, 2])
    Pair = p.sb("Pair", [128, 4096])
    Mflat = p.sb("Mflat", [1, 4096])
    Msb = p.sb("Msb", [128, 64])
    MT_bf = p.sb("MT_bf", [128, 64], BF16)
    S_sb = p.sb("S_sb", [128, 2048])
    S_bf = p.sb("S_bf", [128, 2048], BF16)
    y_sub = p.sb("y_sub", [128, 2048])
    y_sub_bf = p.sb("y_sub_bf", [128, 2048], BF16)
    yT = p.sb("yT", [128, 16 * 128], BF16)
    out_sb = p.sb("out_sb", [128, D])
    mscr = p.scratch("mscr", [4096])

    p.memset(S_sb[:], 0.0)
    p.memset(S_bf[:], 0.0)
    for j in range(16):
        p.memset(convbuf[:, j * 132: j * 132 + 4], 0.0)

    for c in range(NCH):
        r0 = c * 128
        p.dma(x_sb[:], x_h[r0:r0 + 128, :])
        p.ln(x_sb, xn, xn_bf, scr, col)
        p.tr_chunk(xnT, xn_bf, ps_t)
        for (W_h, dst) in [(wz_h, z_bf), (wxm_h, xm_bf)]:
            for k in range(8):
                p.dma(wstg[:], W_h[k * 128:(k + 1) * 128, :])
                p.scopy(wbuf[:], wstg[:])
                for g in range(4):
                    p.mm(ps_big[:, g * 512:(g + 1) * 512],
                         xnT[:, k * 128:(k + 1) * 128],
                         wbuf[:, g * 512:(g + 1) * 512],
                         start=(k == 0), stop=(k == 7))
            for g in range(4):
                p.act(scr[:, 0:512], ps_big[:, g * 512:(g + 1) * 512], AF.Silu)
                p.ts(dst[:, g * 512:(g + 1) * 512], scr[:, 0:512], 8.0, ALU.min)
        for (W, dst) in [(WB, bT_bf), (WC, cT_bf)]:
            for k in range(8):
                p.mm(ps_f[:], xnT[:, k * 128:(k + 1) * 128],
                     W[:, k * 128:(k + 1) * 128], start=(k == 0), stop=(k == 7))
            p.scopy(scr[:, 0:128], ps_f[:])
            p.scopy(xmcT_bf[:], scr[:, 0:128])
            p.tr(ps_t[:], xmcT_bf[:], p.ident_bf[:])
            p.vcopy(dst[:], ps_t[:])
        for k in range(8):
            p.mm(ps_f[:, 0:1], xnT[:, k * 128:(k + 1) * 128],
                 Wdt[:, k:k + 1], start=(k == 0), stop=(k == 7))
        p.vcopy(dts[:, 0:1], ps_f[:, 0:1])
        p.act(dts[:, 1:2], dts[:, 0:1], AF.Abs)
        p.act(dts[:, 2:3], dts[:, 1:2], AF.Exp, scale=-1.0)
        p.act(dts[:, 3:4], dts[:, 2:3], AF.Ln, bias=1.0)
        p.stt(dts[:, 1:2], dts[:, 0:1], 0.0, dts[:, 3:4], ALU.max, ALU.add)
        for j in range(16):
            p.tr(ps_t[:], xm_bf[:, j * 128:(j + 1) * 128], p.ident_bf[:])
            p.vcopy(convbuf[:, j * 132 + 4: j * 132 + 132], ps_t[:])
            p.ts(scr[:, 0:128], convbuf[:, j * 132 + 1: j * 132 + 129],
                 convw[:, j * 4:j * 4 + 1], ALU.mult)
            for tap in range(1, 4):
                p.stt(scr[:, 0:128], convbuf[:, j * 132 + 1 + tap: j * 132 + 129 + tap],
                      convw[:, j * 4 + tap:j * 4 + tap + 1], scr[:, 0:128],
                      ALU.mult, ALU.add)
            p.act(scr[:, 128:256], scr[:, 0:128], AF.Silu)
            p.ts(xmcT_bf[:], scr[:, 128:256], 8.0, ALU.min)
            p.vcopy(convbuf[:, j * 132 + 1: j * 132 + 4],
                    convbuf[:, j * 132 + 129: j * 132 + 132])
            p.tr(ps_t[:], xmcT_bf[:], p.ident_bf[:])
            p.vcopy(xmc_bf[:, j * 128:(j + 1) * 128], ps_t[:])
        p.mm(ps_f[:, 0:1], blk64[:], dts[:, 1:2], start=True, stop=True)
        p.vcopy(col[:, 4:5], ps_f[:, 0:1])
        p.tr(ps_f[0:1, 0:128], col[:, 4:5], p.ident_f[:])
        p.vcopy(Tb[0:1, :], ps_f[0:1, 0:128])
        p.mm(ps_big[:, 0:128], p.ones_r_f[:], Tb[0:1, :], start=True, stop=True)
        p.ts(U_sb[:], ps_big[:, 0:128], A_col[:, 0:1], ALU.mult)
        p.tr(ps_f[0:1, 0:128], dts[:, 1:2], p.ident_f[:])
        p.vcopy(Tb[0:1, :], ps_f[0:1, 0:128])
        p.mm(ps_big[:, 0:128], p.ones_r_f[:], Tb[0:1, :], start=True, stop=True)
        p.vcopy(dts_bc[:], ps_big[:, 0:128])
        for sc in range(2):
            s0 = 64 * sc
            p.act(Ec[:, s0:s0 + 64], U_sb[:, s0:s0 + 64], AF.Exp)
            p.act(Edec[:, s0:s0 + 64], U_sb[:, s0:s0 + 64], AF.Exp, scale=-1.0,
                  bias=U_sb[:, s0 + 63:s0 + 64])
            p.act(e_end[:, sc:sc + 1], U_sb[:, s0 + 63:s0 + 64], AF.Exp)
        p.tt(CtT[:], cT_bf[:], Ec[:], ALU.mult)
        p.tt(scr[:, 0:128], bT_bf[:], Edec[:], ALU.mult)
        p.tt(BtT[:], scr[:, 0:128], dts_bc[:], ALU.mult)
        p.tr(ps_t[:], BtT[:], p.ident_bf[:])
        p.vcopy(BtTt[:], ps_t[:])
        for sc in range(2):
            s0 = 64 * sc
            rws = slice(s0, s0 + 64)
            ut = _view(U_sb[:], s0, [[0, 64], [1, 64]])
            uu = _view(U_sb[:], s0, [[1, 64], [0, 64]])
            p.tt(Pair[:], ut, uu, ALU.subtract)
            p.tt(Pair[:], Pair[:], maskneg[:], ALU.add)
            p.act(Pair[:], Pair[:], AF.Exp)
            cv = _view(cT_bf[:], s0, [[0, 64], [1, 64]])
            p.tt(Pair[:], Pair[:], cv, ALU.mult)
            bv = _view(bT_bf[:], s0, [[1, 64], [0, 64]])
            p.tt(Pair[:], Pair[:], bv, ALU.mult)
            dv = _view(dts_bc[:], s0, [[1, 64], [0, 64]])
            p.tt(Pair[:], Pair[:], dv, ALU.mult)
            for g in range(8):
                p.mm(ps_q[0:1, 0:512], p.ccol(1.0),
                     Pair[:, g * 512:(g + 1) * 512], start=True, stop=True)
                p.vcopy(Mflat[0:1, g * 512:(g + 1) * 512], ps_q[0:1, 0:512])
            p.dma(mscr.rearrange("(a b) -> a b", a=1), Mflat[0:1, :])
            p.dma(Msb[rws, :], mscr.rearrange("(u t) -> u t", u=64))
            p.scopy(MT_bf[rws, :], Msb[rws, :])
            for g in range(4):
                gs = slice(g * 512, (g + 1) * 512)
                p.mm(ps_big[rws, gs], MT_bf[rws, :], xmc_bf[rws, gs],
                     start=True, stop=False)
                p.mm(ps_big[rws, gs], CtT[:, rws], S_bf[:, gs],
                     start=False, stop=True)
            p.tt(y_sub[rws, :], xmc_bf[rws, :], Dp_bc[rws, :], ALU.mult)
            p.tt(y_sub[rws, :], y_sub[rws, :], ps_big[rws, :], ALU.add)
            for g in range(4):
                gs = slice(g * 512, (g + 1) * 512)
                p.mm(ps_big[:, gs], BtTt[rws, :], xmc_bf[rws, gs],
                     start=True, stop=True)
            p.stt(S_sb[:], S_sb[:], e_end[:, sc:sc + 1], ps_big[:], ALU.mult, ALU.add)
            p.scopy(S_bf[:], S_sb[:])
            p.act(scr[rws, 0:2048], y_sub[rws, :], AF.Square,
                  accum_out=col[rws, 6:7])
            p.act(col[rws, 7:8], col[rws, 6:7], AF.Ln, scale=1.0 / 2048,
                  bias=p.ccol(1e-5, s0, s0 + 64))
            p.act(col[rws, 6:7], col[rws, 7:8], AF.Exp, scale=-0.5)
            p.ts(y_sub[rws, :], y_sub[rws, :], col[rws, 6:7], ALU.mult)
            p.tt(y_sub[rws, :], y_sub[rws, :], Nw_bc[rws, :], ALU.mult)
            p.tt(y_sub[rws, :], y_sub[rws, :], z_bf[rws, :], ALU.mult)
            p.scopy(y_sub_bf[rws, :], y_sub[rws, :])
            for k in range(16):
                p.tr(ps_t[:, 0:64], y_sub_bf[rws, k * 128:(k + 1) * 128],
                     p.ident_bf[s0:s0 + 64, s0:s0 + 64])
                p.vcopy(yT[:, k * 128 + s0: k * 128 + s0 + 64], ps_t[:, 0:64])
        for k in range(16):
            p.dma(wstg[:, 0:1024], op_h[k * 128:(k + 1) * 128, :])
            p.scopy(wbuf[:, 0:1024], wstg[:, 0:1024])
            for j2 in range(2):
                p.mm(ps_big[:, j2 * 512:(j2 + 1) * 512],
                     yT[:, k * 128:(k + 1) * 128],
                     wbuf[:, j2 * 512:(j2 + 1) * 512],
                     start=(k == 0), stop=(k == 15))
        for j2 in range(2):
            p.tt(out_sb[:, j2 * 512:(j2 + 1) * 512],
                 ps_big[:, j2 * 512:(j2 + 1) * 512],
                 x_sb[:, j2 * 512:(j2 + 1) * 512], ALU.add)
        p.dma(y_h[r0:r0 + 128, :], out_sb[:])
    return p


def build_L1b(xs, ys, wgT_a, wuT_a, wdT_a):
    p = Prog()
    x_h = p.inp("x", xs)
    y_h = p.inp("yin", ys)
    wg_h = p.inp("wg", wgT_a)
    wu_h = p.inp("wu", wuT_a)
    wd_h = p.inp("wd", wdT_a)
    o_h = p.outp("y", [T, D])
    p.consts()
    H = 3072
    ps_t = p.ps("ps_t", [128, 128], BF16)
    ps_g = p.ps("ps_g", [128, 512])
    ps_u = p.ps("ps_u", [128, 512])

    istg = p.sb("istg", [128, 1536], I32)

    def load_int_bf(name, hbm, R, C):
        nr = R // 128
        dst = p.sb(name, [128, nr * C], BF16)
        for j in range(nr):
            for hh in range(0, C, 1536):
                w = min(1536, C - hh)
                p.dma(istg[:, 0:w], hbm[j * 128:(j + 1) * 128, hh:hh + w])
                p.vcopy(dst[:, j * C + hh:j * C + hh + w], istg[:, 0:w])
        return dst

    WG = load_int_bf("WG", wg_h, 1024, H)
    WU = load_int_bf("WU", wu_h, 1024, H)
    WD = load_int_bf("WD", wd_h, H, 1024)

    x_sb = p.sb("x_sb", [128, D])
    y_sb = p.sb("y_sb", [128, D])
    xn = p.sb("xn", [128, D])
    xn_bf = p.sb("xn_bf", [128, D], BF16)
    scr = p.sb("scr", [128, D])
    col = p.sb("col", [128, 4])
    xnT = p.sb("xnT", [128, D], BF16)
    h_bf = p.sb("h_bf", [128, H], BF16)
    hT = p.sb("hT", [128, H], BF16)
    out_sb = p.sb("out_sb", [128, D])
    SC = 1.0 / 127.0

    for c in range(NCH):
        r0 = c * 128
        p.dma(x_sb[:], x_h[r0:r0 + 128, :])
        p.dma(y_sb[:], y_h[r0:r0 + 128, :])
        p.ln(x_sb, xn, xn_bf, scr, col)
        p.tr_chunk(xnT, xn_bf, ps_t)
        for g in range(6):
            for k in range(8):
                p.mm(ps_g[:], xnT[:, k * 128:(k + 1) * 128],
                     WG[:, k * H + g * 512: k * H + (g + 1) * 512],
                     start=(k == 0), stop=(k == 7))
            for k in range(8):
                p.mm(ps_u[:], xnT[:, k * 128:(k + 1) * 128],
                     WU[:, k * H + g * 512: k * H + (g + 1) * 512],
                     start=(k == 0), stop=(k == 7))
            p.act(h_bf[:, g * 512:(g + 1) * 512], ps_g[:], AF.Silu, scale=SC)
            p.stt(h_bf[:, g * 512:(g + 1) * 512], ps_u[:], SC,
                  h_bf[:, g * 512:(g + 1) * 512], ALU.mult, ALU.mult)
        p.tr_chunk(hT, h_bf, ps_t, n=24)
        for j2 in range(2):
            for k in range(24):
                p.mm(ps_g[:], hT[:, k * 128:(k + 1) * 128],
                     WD[:, k * 1024 + j2 * 512: k * 1024 + (j2 + 1) * 512],
                     start=(k == 0), stop=(k == 23))
            p.stt(out_sb[:, j2 * 512:(j2 + 1) * 512], ps_g[:], SC,
                  y_sb[:, j2 * 512:(j2 + 1) * 512], ALU.mult, ALU.add)
        p.dma(o_h[r0:r0 + 128, :], out_sb[:])
    return p


def build_L2a(xs, Wq_a, Wk_a, Wv_a, Wo_a, bq_a, bk_a, bv_a, bo_a):
    p = Prog()
    x_h = p.inp("x", xs)
    wq_h = p.inp("wq", Wq_a)
    wk_h = p.inp("wk", Wk_a)
    wv_h = p.inp("wv", Wv_a)
    wo_h = p.inp("wo", Wo_a)
    bq_h = p.inp("bq", bq_a)
    bk_h = p.inp("bk", bk_a)
    bv_h = p.inp("bv", bv_a)
    bo_h = p.inp("bo", bo_a)
    y_h = p.outp("y", [T, D])
    p.consts()
    E = 2048
    ps_t = p.ps("ps_t", [128, 128], BF16)
    ps_kt = p.ps("ps_kt", [128, 128])
    ps_v = p.ps("ps_v", [128, 512])
    ps_num = p.ps("ps_num", [128, 2048])
    ps_den = p.ps("ps_den", [128, 1])

    x_sb = p.sb("x_sb", [128, D])
    xn = p.sb("xn", [128, D])
    xn_bf = p.sb("xn_bf", [128, D], BF16)
    scr = p.sb("scr", [128, D])
    col = p.sb("col", [128, 4])
    xnT = p.sb("xnT", [128, D], BF16)
    KT = p.sb("KT", [128, 16 * T], BF16)
    QT = p.sb("QT", [128, 16 * 128], BF16)
    bq_sb = p.sb("bq_sb", [128, 16])
    bk_sb = p.sb("bk_sb", [128, 16])
    bv_bc = p.sb("bv_bc", [128, E])
    bo_bc = p.sb("bo_bc", [128, D])
    brow = p.sb("brow", [1, 2048])
    v_bf = p.sb("v_bf", [128, E], BF16)
    sc_bf = p.sb("sc_bf", [128, 128], BF16)
    vu_bf = v_bf
    o_bf = p.sb("o_bf", [128, E], BF16)
    oT = p.sb("oT", [128, E], BF16)
    den_sb = p.sb("den_sb", [128, 1])
    out_sb = p.sb("out_sb", [128, D])
    vscr = p.scratch("vscr", [T, E], BF16)

    p.dma(bq_sb[:], bq_h)
    p.dma(bk_sb[:], bk_h)
    p.dma(brow[:], bv_h)
    for g in range(4):
        p.mm(ps_v[:], p.ones_r_f[:], brow[0:1, g * 512:(g + 1) * 512])
        p.vcopy(bv_bc[:, g * 512:(g + 1) * 512], ps_v[:])
    p.dma(brow[0:1, 0:1024], bo_h)
    for g in range(2):
        p.mm(ps_v[:], p.ones_r_f[:], brow[0:1, g * 512:(g + 1) * 512])
        p.vcopy(bo_bc[:, g * 512:(g + 1) * 512], ps_v[:])

    w8a = p.load_bf("w8a", wk_h, 1024, E)
    w8b = p.load_bf("w8b", wv_h, 1024, E)

    def phi_evac(dst, s, bcol):
        p.ts(scr[:, 0:128], ps_kt[:], bcol, ALU.add, 0.0, ALU.min)
        p.act(scr[:, 128:256], scr[:, 0:128], AF.Exp)
        p.ts(scr[:, 0:128], ps_kt[:], bcol, ALU.add, 0.0, ALU.max)
        p.tt(dst, scr[:, 0:128], scr[:, 128:256], ALU.add)

    for c in range(NCH):
        r0 = c * 128
        p.dma(x_sb[:], x_h[r0:r0 + 128, :])
        p.ln(x_sb, xn, xn_bf, scr, col)
        p.tr_chunk(xnT, xn_bf, ps_t)
        for s in range(16):
            for k in range(8):
                p.mm(ps_kt[:], w8a[:, k * E + s * 128: k * E + (s + 1) * 128],
                     xnT[:, k * 128:(k + 1) * 128], start=(k == 0), stop=(k == 7))
            phi_evac(KT[:, s * T + r0: s * T + r0 + 128], s, bk_sb[:, s:s + 1])
        for g in range(4):
            for k in range(8):
                p.mm(ps_v[:], xnT[:, k * 128:(k + 1) * 128],
                     w8b[:, k * E + g * 512: k * E + (g + 1) * 512],
                     start=(k == 0), stop=(k == 7))
            p.tt(v_bf[:, g * 512:(g + 1) * 512], ps_v[:],
                 bv_bc[:, g * 512:(g + 1) * 512], ALU.add)
        p.dma(vscr[r0:r0 + 128, :], v_bf[:])
    stg = p.sb("stg2", [128, 1024], F32)
    for j in range(8):
        for hh in range(2):
            p.dma(stg[:], wq_h[j * 128:(j + 1) * 128, hh * 1024:(hh + 1) * 1024])
            p.scopy(w8a[:, j * E + hh * 1024:j * E + (hh + 1) * 1024], stg[:])
    for j in range(16):
        p.dma(stg[:], wo_h[j * 128:(j + 1) * 128, :])
        p.scopy(w8b[:, j * 1024:(j + 1) * 1024], stg[:])
    for c in range(NCH):
        r0 = c * 128
        p.dma(x_sb[:], x_h[r0:r0 + 128, :])
        p.ln(x_sb, xn, xn_bf, scr, col)
        p.tr_chunk(xnT, xn_bf, ps_t)
        for s in range(16):
            for k in range(8):
                p.mm(ps_kt[:], w8a[:, k * E + s * 128: k * E + (s + 1) * 128],
                     xnT[:, k * 128:(k + 1) * 128], start=(k == 0), stop=(k == 7))
            phi_evac(QT[:, s * 128:(s + 1) * 128], s, bq_sb[:, s:s + 1])
        for j in range(c + 1):
            u0 = j * 128
            p.dma(vu_bf[:], vscr[u0:u0 + 128, :])
            for s in range(16):
                p.mm(ps_kt[:], KT[:, s * T + u0: s * T + u0 + 128],
                     QT[:, s * 128:(s + 1) * 128], start=(s == 0), stop=(s == 15))
            if j == c:
                p.tt(sc_bf[:], ps_kt[:], p.triu_bf[:], ALU.mult)
            else:
                p.scopy(sc_bf[:], ps_kt[:])
            for g in range(4):
                p.mm(ps_num[:, g * 512:(g + 1) * 512], sc_bf[:],
                     vu_bf[:, g * 512:(g + 1) * 512], start=(j == 0), stop=(j == c))
            p.mm(ps_den[:], sc_bf[:], p.ones_c_bf[:], start=(j == 0), stop=(j == c))
        p.ts(den_sb[:], ps_den[:], 1e-6, ALU.add)
        p.recip(den_sb[:], den_sb[:])
        for g in range(4):
            p.ts(o_bf[:, g * 512:(g + 1) * 512], ps_num[:, g * 512:(g + 1) * 512],
                 den_sb[:, 0:1], ALU.mult)
        p.tr_chunk(oT, o_bf, ps_t, n=16)
        for j2 in range(2):
            for k in range(16):
                p.mm(ps_num[:, 0:512], oT[:, k * 128:(k + 1) * 128],
                     w8b[:, k * 1024 + j2 * 512: k * 1024 + (j2 + 1) * 512],
                     start=(k == 0), stop=(k == 15))
            p.tt(out_sb[:, j2 * 512:(j2 + 1) * 512], ps_num[:, 0:512],
                 bo_bc[:, j2 * 512:(j2 + 1) * 512], ALU.add)
        p.tt(out_sb[:], out_sb[:], x_sb[:], ALU.add)
        p.dma(y_h[r0:r0 + 128, :], out_sb[:])
    return p


def build_L3a(xs, wqkv_a, proj_a, wout_a, bout_a, wr_a):
    p = Prog()
    x_h = p.inp("x", xs)
    wq_h = p.inp("wqkv", wqkv_a)
    pj_h = p.inp("proj", proj_a)
    wo_h = p.inp("wout", wout_a)
    bo_h = p.inp("bout", bout_a)
    wr_h = p.inp("wr", wr_a)
    y_h = p.outp("y", [T, D])
    xn_h = p.outp("xn3", [T, D])
    sc_h = p.outp("scores", [T, 8])
    p.consts()
    ps_t = p.ps("ps_t", [128, 128], BF16)
    ps_q = p.ps("ps_q", [128, 512])
    ps_kp = p.ps("ps_kp", [128, 256])
    ps_kv = p.ps("ps_kv", [128, 2048])
    ps_ks = p.ps("ps_ks", [128, 32])

    WQ = p.load_bf("WQ", wq_h, 1024, 3072)
    WO = p.load_bf("WO", wo_h, 1024, 1024)
    WR = p.load_bf("WR", wr_h, 1024, 8)
    pj_stg = p.sb("pj_stg", [64, 4096], F32)
    PJ = p.sb("PJ", [64, 4096], BF16)
    p.dma(pj_stg[:], pj_h)
    p.scopy(PJ[:], pj_stg[:])
    bo_row = p.sb("bo_row", [1, 1024], BF16)
    bo_stg = p.sb("bo_stg", [1, 1024], F32)
    p.dma(bo_stg[:], bo_h)
    p.scopy(bo_row[:], bo_stg[:])

    x_sb = p.sb("x_sb", [128, D])
    xn = p.sb("xn", [128, D])
    xn_bf = p.sb("xn_bf", [128, D], BF16)
    scr = p.sb("scr", [128, 1024])
    col = p.sb("col", [128, 4])
    xnT = p.sb("xnT", [128, D], BF16)
    qkv_bf = p.sb("qkv_bf", [128, 3072], BF16)
    kT_h = p.sb("kT_h", [64, 128], BF16)
    kp_bf = p.sb("kp_bf", [128, 256], BF16)
    kv_sb = p.sb("kv_sb", [128, 2048], BF16)
    ks_bf = p.sb("ks_bf", [128, 32], BF16)
    qpT_bf = p.sb("qpT_bf", [128, 256], BF16)
    attn_bf = p.sb("attn_bf", [128, 1024], BF16)
    attnT = p.sb("attnT", [128, 1024], BF16)
    dcol = p.sb("dcol", [128, 2])
    drow = p.sb("drow", [1, 128], BF16)
    out_sb = p.sb("out_sb", [128, D])
    sc_sb = p.sb("sc_sb", [128, 8])

    def elu1(dst, src_ps, w=256):
        p.ts(scr[:, 0:w], src_ps, 0.0, ALU.min)
        p.act(scr[:, 256:256 + w], scr[:, 0:w], AF.Exp)
        p.ts(scr[:, 0:w], src_ps, 0.0, ALU.max)
        p.tt(dst, scr[:, 0:w], scr[:, 256:256 + w], ALU.add)

    for c in range(NCH):
        r0 = c * 128
        p.dma(x_sb[:], x_h[r0:r0 + 128, :])
        p.ln(x_sb, xn, xn_bf, scr, col)
        p.tr_chunk(xnT, xn_bf, ps_t)
        for g in range(6):
            for k in range(8):
                p.mm(ps_q[:], xnT[:, k * 128:(k + 1) * 128],
                     WQ[:, k * 3072 + g * 512: k * 3072 + (g + 1) * 512],
                     start=(k == 0), stop=(k == 7))
            p.vcopy(qkv_bf[:, g * 512:(g + 1) * 512], ps_q[:])
        for h in range(16):
            ks = slice(1024 + 64 * h, 1024 + 64 * h + 64)
            vs = slice(2048 + 64 * h, 2048 + 64 * h + 64)
            p.tr(ps_t[0:64, 0:128], qkv_bf[:, ks], p.ident_bf[:])
            p.vcopy(kT_h[:], ps_t[0:64, 0:128])
            p.mm(ps_kp[:], kT_h[:], PJ[:, 256 * h:256 * (h + 1)], start=True, stop=True)
            elu1(kp_bf[:], ps_kp[:])
            for half in range(2):
                hs = slice(128 * half, 128 * half + 128)
                p.mm(ps_kv[:, (2 * h + half) * 64:(2 * h + half + 1) * 64],
                     kp_bf[:, hs], qkv_bf[:, vs], start=(c == 0), stop=(c == NCH - 1))
                p.mm(ps_ks[:, (2 * h + half):(2 * h + half) + 1],
                     kp_bf[:, hs], p.ones_c_bf[:], start=(c == 0), stop=(c == NCH - 1))
    p.vcopy(kv_sb[:], ps_kv[:])
    p.vcopy(ks_bf[:], ps_ks[:])
    for c in range(NCH):
        r0 = c * 128
        p.dma(x_sb[:], x_h[r0:r0 + 128, :])
        p.ln(x_sb, xn, xn_bf, scr, col)
        p.dma(xn_h[r0:r0 + 128, :], xn[:])
        p.tr_chunk(xnT, xn_bf, ps_t)
        for k in range(8):
            p.mm(ps_kp[:, 0:8], xnT[:, k * 128:(k + 1) * 128],
                 WR[:, k * 8:(k + 1) * 8], start=(k == 0), stop=(k == 7))
        p.rmax(col[:, 0:1], ps_kp[:, 0:8])
        p.ts(col[:, 1:2], col[:, 0:1], -1.0, ALU.mult)
        p.act(sc_sb[:], ps_kp[:, 0:8], AF.Exp, bias=col[:, 1:2])
        p.rsum(col[:, 2:3], sc_sb[:])
        p.recip(col[:, 3:4], col[:, 2:3])
        p.ts(sc_sb[:], sc_sb[:], col[:, 3:4], ALU.mult)
        p.dma(sc_h[r0:r0 + 128, :], sc_sb[:])
        for g in range(2):
            for k in range(8):
                p.mm(ps_q[:], xnT[:, k * 128:(k + 1) * 128],
                     WQ[:, k * 3072 + g * 512: k * 3072 + (g + 1) * 512],
                     start=(k == 0), stop=(k == 7))
            p.vcopy(qkv_bf[:, g * 512:(g + 1) * 512], ps_q[:])
        for h in range(16):
            qs = slice(64 * h, 64 * h + 64)
            p.tr(ps_t[0:64, 0:128], qkv_bf[:, qs], p.ident_bf[:])
            p.vcopy(kT_h[:], ps_t[0:64, 0:128])
            for half in range(2):
                p.mm(ps_kp[:, 0:128], PJ[:, 256 * h + 128 * half:256 * h + 128 * (half + 1)],
                     kT_h[:], start=True, stop=True)
                elu1(qpT_bf[:, 128 * half:128 * (half + 1)], ps_kp[:, 0:128], w=128)
            for half in range(2):
                p.mm(ps_kp[0:1, 0:128],
                     ks_bf[:, (2 * h + half):(2 * h + half) + 1],
                     qpT_bf[:, 128 * half:128 * (half + 1)],
                     start=(half == 0), stop=(half == 1))
            p.vcopy(drow[0:1, :], ps_kp[0:1, 0:128])
            p.tr(ps_t[:, 0:1], drow[0:1, :], p.ident_bf[0:1, 0:1])
            p.vcopy(dcol[:, 0:1], ps_t[:, 0:1])
            p.ts(dcol[:, 1:2], dcol[:, 0:1], 1e-8, ALU.add)
            p.recip(dcol[:, 0:1], dcol[:, 1:2])
            for half in range(2):
                p.mm(ps_kp[:, 128:192], qpT_bf[:, 128 * half:128 * (half + 1)],
                     kv_sb[:, (2 * h + half) * 64:(2 * h + half + 1) * 64],
                     start=(half == 0), stop=(half == 1))
            p.ts(attn_bf[:, qs], ps_kp[:, 128:192], dcol[:, 0:1], ALU.mult)
        p.tr_chunk(attnT, attn_bf, ps_t)
        for j2 in range(2):
            for k in range(8):
                p.mm(ps_q[:], attnT[:, k * 128:(k + 1) * 128],
                     WO[:, k * 1024 + j2 * 512: k * 1024 + (j2 + 1) * 512],
                     start=(k == 0), stop=False)
            p.mm(ps_q[:], p.ones_r_bf[:], bo_row[0:1, j2 * 512:(j2 + 1) * 512],
                 start=False, stop=True)
            p.tt(out_sb[:, j2 * 512:(j2 + 1) * 512], ps_q[:],
                 x_sb[:, j2 * 512:(j2 + 1) * 512], ALU.add)
        p.dma(y_h[r0:r0 + 128, :], out_sb[:])
    return p


def build_L3b(y3a_s, xn3_s, scT_a, scown_s, W1_a, b1_a, W2_a, b2_a, E8_a, E8T_a):
    p = Prog()
    y_h = p.inp("y3a", y3a_s)
    xn_h = p.inp("xn3", xn3_s)
    st_h = p.inp("scT", scT_a)
    so_h = p.inp("scown", scown_s)
    w1_h = p.inp("w1", W1_a)
    b1_h = p.inp("b1", b1_a)
    w2_h = p.inp("w2", W2_a)
    b2_h = p.inp("b2", b2_a)
    e8_h = p.inp("e8", E8_a)
    e8t_h = p.inp("e8t", E8T_a)
    o_h = p.outp("y", [T, D])
    p.consts()
    K = 4096
    ps_t = p.ps("ps_t", [128, 128], BF16)
    ps_up = p.ps("ps_up", [128, 512])
    ps_dn = p.ps("ps_dn", [128, 512])
    ps_b = p.ps("ps_b", [128, 8])

    ctr_sb = p.sb("ctr_sb", [128, 1024])
    xn_sb = p.sb("xn_sb", [128, 1024])
    SS = ctr_sb
    p.dma(SS[:], st_h)
    E8 = p.sb("E8", [8, 128])
    p.dma(E8[:], e8_h)
    E8T = p.sb("E8T", [128, 8])
    p.dma(E8T[:], e8t_h)
    b8 = p.sb("b8", [8, 8])
    cnt128 = p.sb("cnt128", [128, 2])
    scr = xn_sb
    ge8 = p.sb("ge8", [8, 1], U8)
    p.memset(b8[:, 0:1], 0.0)
    p.memset(b8[:, 1:2], 1.0)
    for it in range(48):
        p.tt(b8[:, 2:3], b8[:, 0:1], b8[:, 1:2], ALU.add)
        p.ts(b8[:, 2:3], b8[:, 2:3], 0.5, ALU.mult)
        p.mm(ps_b[:, 0:1], E8[:], b8[:, 2:3], start=True, stop=True)
        p.vcopy(cnt128[:, 0:1], ps_b[:, 0:1])
        p.ts(scr[:], SS[:], cnt128[:, 0:1], ALU.is_gt, accum_out=cnt128[:, 1:2],
             op1=ALU.add)
        p.mm(ps_b[0:8, 0:1], E8T[:], cnt128[:, 1:2], start=True, stop=True)
        p.ts(ge8[:], ps_b[0:8, 0:1], float(K), ALU.is_ge)
        p.sel(b8[:, 4:5], ge8[:], b8[:, 2:3], b8[:, 0:1])
        p.sel(b8[:, 5:6], ge8[:], b8[:, 1:2], b8[:, 2:3])
        p.vcopy(b8[:, 0:1], b8[:, 4:5])
        p.vcopy(b8[:, 1:2], b8[:, 5:6])
    lo_bc = p.sb("lo_bc", [128, 8])
    loT = p.sb("loT", [1, 8])
    ps_f = p.ps("ps_f", [128, 128])
    lo_st = p.sb("lo_st", [128, 1])
    p.memset(lo_st[:, 0:1], 0.0)
    p.vcopy(lo_st[0:8, 0:1], b8[:, 0:1])
    p.tr(ps_f[0:1, 0:128], lo_st[:, 0:1], p.ident_f[:])
    p.vcopy(loT[0:1, :], ps_f[0:1, 0:8])
    p.mm(ps_f[:, 0:8], p.ones_r_f[:], loT[0:1, :], start=True, stop=True)
    p.vcopy(lo_bc[:], ps_f[:, 0:8])
    w_all = p.sb("w_all", [128, 128])
    sco = p.sb("sco", [128, 8])
    for c in range(NCH):
        p.dma(sco[:], so_h[c * 128:(c + 1) * 128, :])
        p.tt(scr[:, 0:8], sco[:], lo_bc[:], ALU.is_gt)
        p.tt(w_all[:, c * 8:(c + 1) * 8], scr[:, 0:8], sco[:], ALU.mult)
    xnT_all = p.sb("xnT_all", [128, 16 * 1024], BF16)
    h_bf = p.sb("h_bf", [128, 4096], BF16)
    xn_bf = h_bf[:, 0:1024]
    for c in range(NCH):
        p.dma(xn_sb[:], xn_h[c * 128:(c + 1) * 128, :])
        p.scopy(xn_bf, xn_sb[:])
        for k in range(8):
            p.tr(ps_t[:], h_bf[:, k * 128:(k + 1) * 128], p.ident_bf[:])
            p.vcopy(xnT_all[:, c * 1024 + k * 128:c * 1024 + (k + 1) * 128], ps_t[:])
    w1stg = p.sb("w1stg", [128, 1024], F32)
    W1bf = p.sb("W1bf", [128, 8 * 4096], BF16)
    W2bf = p.sb("W2bf", [128, 32 * 1024], BF16)
    b1row = p.sb("b1row", [1, 4096], BF16)
    b2row = p.sb("b2row", [1, 1024], BF16)
    brstg = p.sb("brstg", [1, 1024], F32)
    hT = p.sb("hT", [128, 4096], BF16)
    escr = p.scratch("escr", [8 * T, D])
    for e in range(8):
        for j in range(8):
            for q4 in range(4):
                p.dma(w1stg[:], w1_h[e * 1024 + j * 128: e * 1024 + (j + 1) * 128,
                                     q4 * 1024:(q4 + 1) * 1024])
                p.scopy(W1bf[:, j * 4096 + q4 * 1024:j * 4096 + (q4 + 1) * 1024],
                        w1stg[:])
        for j in range(32):
            p.dma(w1stg[:], w2_h[e * 4096 + j * 128: e * 4096 + (j + 1) * 128, :])
            p.scopy(W2bf[:, j * 1024:(j + 1) * 1024], w1stg[:])
        for q4 in range(4):
            p.dma(brstg[:], b1_h[e:e + 1, q4 * 1024:(q4 + 1) * 1024])
            p.scopy(b1row[0:1, q4 * 1024:(q4 + 1) * 1024], brstg[:])
        p.dma(brstg[:], b2_h[e:e + 1, :])
        p.scopy(b2row[:], brstg[:])
        for c in range(NCH):
            for g in range(8):
                for k in range(8):
                    p.mm(ps_up[:], xnT_all[:, c * 1024 + k * 128:c * 1024 + (k + 1) * 128],
                         W1bf[:, k * 4096 + g * 512: k * 4096 + (g + 1) * 512],
                         start=(k == 0), stop=False)
                p.mm(ps_up[:], p.ones_r_bf[:], b1row[0:1, g * 512:(g + 1) * 512],
                     start=False, stop=True)
                p.act(h_bf[:, g * 512:(g + 1) * 512], ps_up[:], AF.Gelu)
            p.tr_chunk(hT, h_bf, ps_t, n=32)
            for j2 in range(2):
                for k in range(32):
                    p.mm(ps_dn[:], hT[:, k * 128:(k + 1) * 128],
                         W2bf[:, k * 1024 + j2 * 512: k * 1024 + (j2 + 1) * 512],
                         start=(k == 0), stop=False)
                p.mm(ps_dn[:], p.ones_r_bf[:], b2row[0:1, j2 * 512:(j2 + 1) * 512],
                     start=False, stop=True)
                p.ts(ctr_sb[:, j2 * 512:(j2 + 1) * 512], ps_dn[:],
                     w_all[:, c * 8 + e: c * 8 + e + 1], ALU.mult)
            p.dma(escr[e * T + c * 128: e * T + (c + 1) * 128, :], ctr_sb[:])
    acc = ctr_sb
    ld = xn_sb
    for c in range(NCH):
        r0 = c * 128
        p.dma(acc[:], y_h[r0:r0 + 128, :])
        for e in range(8):
            p.dma(ld[:], escr[e * T + r0: e * T + r0 + 128, :])
            p.tt(acc[:], acc[:], ld[:], ALU.add)
        p.dma(o_h[r0:r0 + 128, :], acc[:])
    return p


def kernel(**inputs):
    f32 = np.float32
    x = np.ascontiguousarray(inputs["L0_attn_x"].astype(f32, copy=False))
    xs = shard(x)

    W0T = np.ascontiguousarray(inputs["L0_attn_w"].T)
    p = build_L0a(xs, W0T, np.ascontiguousarray(inputs["L0_attn_w_out"]))
    r = p.run()
    y0a = [r[i]["y"] for i in range(NCORES)]

    p = build_fp4_ffn(xs, y0a, np.ascontiguousarray(inputs["L0_ffn_w_gate"]),
                      np.ascontiguousarray(inputs["L0_ffn_w_up"]),
                      np.ascontiguousarray(inputs["L0_ffn_w_down"]), "l0b")
    r = p.run()
    x1 = [r[i]["y"] for i in range(NCORES)]

    ip = inputs["L1_attn_in_proj_w"]
    WzT = np.ascontiguousarray(ip[0:2048].T)
    WxmT = np.ascontiguousarray(ip[2048:4096].T)
    WBT = np.ascontiguousarray(ip[4096:4224].T)
    WCT = np.ascontiguousarray(ip[4224:4352].T)
    wdtT = np.ascontiguousarray(ip[4352:4353].T)
    convw = np.ascontiguousarray(inputs["L1_attn_conv_w"].reshape(2048, 4))
    OPT = np.ascontiguousarray(inputs["L1_attn_out_proj_w"].T)
    maskneg = np.zeros((64, 64), f32)
    for u in range(64):
        for t in range(64):
            if t < u:
                maskneg[u, t] = -1e30
    maskneg = np.tile(maskneg.reshape(1, 4096), (128, 1)).astype(BF)
    blk64 = np.zeros((128, 128), f32)
    for u in range(128):
        for t in range(128):
            if (u // 64 == t // 64) and u <= t:
                blk64[u, t] = 1.0
    p = build_L1a(x1, WzT, WxmT, WBT, WCT, wdtT, convw,
                  np.ascontiguousarray(inputs["L1_attn_A_log"]),
                  np.ascontiguousarray(inputs["L1_attn_Dp"]),
                  np.ascontiguousarray(inputs["L1_attn_norm_w"]), OPT, maskneg, blk64)
    r = p.run()
    y1a = [r[i]["y"] for i in range(NCORES)]

    p = build_L1b(x1, y1a, np.ascontiguousarray(inputs["L1_ffn_w_gate"].T),
                  np.ascontiguousarray(inputs["L1_ffn_w_up"].T),
                  np.ascontiguousarray(inputs["L1_ffn_w_down"].T))
    r = p.run()
    x2 = [r[i]["y"] for i in range(NCORES)]

    rb = lambda b: np.ascontiguousarray(b.reshape(16, 128).T)
    p = build_L2a(x2, np.ascontiguousarray(inputs["L2_attn_Wq"]),
                  np.ascontiguousarray(inputs["L2_attn_Wk"]),
                  np.ascontiguousarray(inputs["L2_attn_Wv"]),
                  np.ascontiguousarray(inputs["L2_attn_Wo"]),
                  rb(inputs["L2_attn_bq"]), rb(inputs["L2_attn_bk"]),
                  np.ascontiguousarray(inputs["L2_attn_bv"].reshape(1, 2048)),
                  np.ascontiguousarray(inputs["L2_attn_bo"].reshape(1, 1024)))
    r = p.run()
    y2a = [r[i]["y"] for i in range(NCORES)]

    p = build_fp4_ffn(x2, y2a, np.ascontiguousarray(inputs["L2_ffn_w_gate"]),
                      np.ascontiguousarray(inputs["L2_ffn_w_up"]),
                      np.ascontiguousarray(inputs["L2_ffn_w_down"]), "l2b")
    r = p.run()
    x3 = [r[i]["y"] for i in range(NCORES)]

    proj = np.ascontiguousarray(
        inputs["L3_attn_proj_matrix"].transpose(1, 0, 2).reshape(64, 16 * 256))
    p = build_L3a(x3, np.ascontiguousarray(inputs["L3_attn_w_qkv"]), proj,
                  np.ascontiguousarray(inputs["L3_attn_w_out"]),
                  np.ascontiguousarray(inputs["L3_attn_b_out"].reshape(1, 1024)),
                  np.ascontiguousarray(inputs["L3_ffn_Wr"]))
    r = p.run()
    y3a = [r[i]["y"] for i in range(NCORES)]
    xn3 = [r[i]["xn3"] for i in range(NCORES)]
    scores = [r[i]["scores"] for i in range(NCORES)]

    sc_full = np.concatenate(scores, axis=0)
    scT = np.ascontiguousarray(sc_full.T).reshape(128, 1024)
    E8 = np.zeros((8, 128), f32)
    for e in range(8):
        E8[e, e * 16:(e + 1) * 16] = 1.0
    E8T = np.ascontiguousarray(E8.T)
    p = build_L3b(y3a, xn3, scT, scores,
                  np.ascontiguousarray(inputs["L3_ffn_W1"].reshape(8192, 4096)),
                  np.ascontiguousarray(inputs["L3_ffn_b1"]),
                  np.ascontiguousarray(inputs["L3_ffn_W2"].reshape(32768, 1024)),
                  np.ascontiguousarray(inputs["L3_ffn_b2"]), E8, E8T)
    r = p.run()
    out = np.stack([r[i]["y"] for i in range(NCORES)], axis=0)
    return out.astype(np.float32)
